# version 05
# baseline (speedup 1.0000x reference)
import contextlib
import math

import numpy as np
import ml_dtypes

import concourse.bass as bass
import concourse.mybir as mybir
from concourse.bass_utils import run_bass_kernel_spmd

F32 = mybir.dt.float32
BF16 = mybir.dt.bfloat16
AF = mybir.ActivationFunctionType
OP = mybir.AluOpType

NC = 8
T = 8192
D = 4096
TC = T // NC
KT = D // 128
HEADS = 32
HD = 128
KVH = 8
EPS = 1e-5
ISQ = 1.0 / math.sqrt(HD)
BF = ml_dtypes.bfloat16

CHAINED_OPS = frozenset((
    "tensor_copy", "tensor_tensor", "tensor_add", "tensor_sub", "tensor_mul",
    "tensor_max", "tensor_scalar", "tensor_single_scalar", "tensor_scalar_add",
    "tensor_scalar_sub", "tensor_scalar_mul", "tensor_scalar_max",
    "tensor_scalar_min", "tensor_relu", "tensor_reduce", "reduce_max",
    "reduce_sum", "scalar_tensor_tensor", "memset", "memzero", "activation",
    "reciprocal", "select", "copy_predicated", "sign", "square",
    "sqrt", "mul", "copy", "add",
))


class ChainedEngine:
    def __init__(self, h, sem):
        self._h = h
        self._sem = sem
        self._cnt = 0

    def __getattr__(self, name):
        attr = getattr(self._h, name)
        if name not in CHAINED_OPS:
            return attr

        def wrapped(*a, **k):
            if self._cnt:
                self._h.wait_ge(self._sem, self._cnt)
            inst = attr(*a, **k)
            inst.then_inc(self._sem, 1)
            self._cnt += 1
            return None
        return wrapped


class Prog:

    ENGINES = ("sync", "gpsimd", "tensor", "vector", "scalar")
    CHAINED = ("gpsimd", "vector", "scalar")

    def __init__(self):
        self.ops = {e: [] for e in self.ENGINES}
        self.cnt = {e: 0 for e in self.CHAINED}
        self.chs = {}

    def on(self, eng, fn):
        self.ops[eng].append(fn)

    def add(self, eng, nops, fn):
        self.cnt[eng] += nops
        fn._expect_after = self.cnt[eng]
        self.ops[eng].append(fn)
        return self.cnt[eng]

    def emit(self, block, chain_sems):
        self.chs = chain_sems
        for eng in self.ENGINES:
            fns = self.ops[eng]
            if not fns:
                continue

            def body(handle, fns=fns, eng=eng):
                if eng in chain_sems:
                    handle = ChainedEngine(handle, chain_sems[eng])
                for fn in fns:
                    fn(handle)
                    exp = getattr(fn, "_expect_after", None)
                    if exp is not None and isinstance(handle, ChainedEngine):
                        assert handle._cnt == exp, (
                            f"{eng}: after closure {fn} count={handle._cnt} "
                            f"declared={exp}")
                if eng in self.CHAINED and isinstance(handle, ChainedEngine):
                    assert handle._cnt == self.cnt[eng], (
                        f"{eng}: emitted {handle._cnt} chained ops, "
                        f"declared {self.cnt[eng]}")
            getattr(block, eng)(body)


class Sem:
    def __init__(self, st, nc, name):
        self.h = st.enter_context(nc.semaphore(name))
        self.n = 0

    def inc(self, k=1):
        self.n += k
        return self.n


class DmaSlots:

    def __init__(self, st, nc, name, n):
        self.sems = [Sem(st, nc, f"{name}{i}") for i in range(n)]
        self.n = n

    def issue(self, i):
        s = self.sems[i % self.n]
        s.inc(16)
        return s.h

    def wait_val(self, i):
        return self.sems[i % self.n].h, 16 * (i // self.n + 1)


def sbuf(st, nc, name, shape, dtype):
    return st.enter_context(nc.sbuf_tensor(name, shape, dtype))


def psum(st, nc, name, shape, dtype=F32):
    return st.enter_context(nc.psum_tensor(name, shape, dtype))


class Consts:
    def __init__(self, nc, st, prog, s_ld):
        self.nc, self.st, self.prog, self.s_ld = nc, st, prog, s_ld
        self.items = []

    def add(self, name, arr):
        arr = np.ascontiguousarray(arr)
        dt = BF16 if arr.dtype == BF else F32
        dram = self.nc.inline_tensor(arr, name + "_c").ap()
        tile = sbuf(self.st, self.nc, name, list(arr.shape), dt)
        self.items.append((tile, dram))
        return tile

    def emit_loads(self):
        items = self.items

        def loads(sync):
            for tile, dram in items:
                sync.dma_start(out=tile[:], in_=dram).then_inc(self.s_ld.h, 16)
        self.prog.on("sync", loads)
        self.s_ld.inc(16 * len(items))



def emit_ln(P, q, xsrc, xn, scr, C, ps, s_m, stream=None, out_stream=None):
    nqb = q // 512
    v0 = P.cnt["vector"]
    m0 = s_m.n
    streaming = stream is not None
    vcast, asq = [], []

    if streaming:
        buf3, slots = stream

        def slot(i):
            return buf3[:, (i % 3) * q:(i % 3) * q + q]

    for k in range(KT):
        col = slice((k % 2) * q, (k % 2) * q + q)
        if k >= 2:
            need = m0 + 2 * nqb * (k - 1)
            P.on("vector", lambda v, n=need: v.wait_ge(s_m.h, n))
            P.on("scalar", lambda s, n=need: s.wait_ge(s_m.h, n))
        if streaming:
            src = slot(k)
            wh, wv = slots.wait_val(k)
            P.on("vector", lambda v, wh=wh, wv=wv: v.wait_ge(wh, wv))
            P.on("scalar", lambda s, wh=wh, wv=wv: s.wait_ge(wh, wv))
        else:
            src = xsrc[:, k * q:(k + 1) * q]
        vv = P.add("vector", 1, lambda v, src=src, col=col: v.tensor_copy(
            out=scr["lnb"][:, col], in_=src))
        vcast.append(vv)
        aa = P.add("scalar", 1, lambda s, src=src, col=col: s.activation(
            out=scr["lnq"][:, col], in_=src, func=AF.Square))
        asq.append(aa)

        def mm(t, k=k, col=col, vv=vv, aa=aa):
            t.wait_ge(P.chs["vector"], vv)
            t.wait_ge(P.chs["scalar"], aa)
            for qb in range(nqb):
                t.matmul(ps["A"][0:1, qb * 512:(qb + 1) * 512], C["ones_col_bf"][:],
                         scr["lnb"][:, col][:, qb * 512:(qb + 1) * 512],
                         start=(k == 0), stop=(k == KT - 1)).then_inc(s_m.h, 1)
            for qb in range(nqb):
                t.matmul(ps["B"][0:1, qb * 512:(qb + 1) * 512], C["ones_col_bf"][:],
                         scr["lnq"][:, col][:, qb * 512:(qb + 1) * 512],
                         start=(k == 0), stop=(k == KT - 1)).then_inc(s_m.h, 1)
        P.on("tensor", mm)
        s_m.inc(2 * nqb)

    mm_all = s_m.n
    P.on("vector", lambda v, n=mm_all: v.wait_ge(s_m.h, n))

    def stats(v):
        v.tensor_copy(out=scr["st_s"][:, 0:q], in_=ps["A"][0:1, 0:q])
        v.tensor_copy(out=scr["st_s"][:, q:2 * q], in_=ps["B"][0:1, 0:q])
        v.tensor_scalar(out=scr["m_s"][:], in0=scr["st_s"][:, 0:q],
                        scalar1=C["invn"][:], scalar2=None, op0=OP.mult)
        v.tensor_tensor(out=scr["b_s"][:], in0=scr["m_s"][:], in1=scr["m_s"][:],
                        op=OP.mult)
        v.scalar_tensor_tensor(out=scr["v_s"][:], in0=scr["st_s"][:, q:2 * q],
                               scalar=C["invn"][:], in1=scr["b_s"][:],
                               op0=OP.mult, op1=OP.subtract)
    va = P.add("vector", 5, stats)

    P.on("scalar", lambda s, n=va: s.wait_ge(P.chs["vector"], n))

    def rstd(s):
        s.activation(out=scr["r_s"][:], in_=scr["v_s"][:], func=AF.Ln,
                     bias=C["eps"][0:1, :])
        s.activation(out=scr["r_s"][:], in_=scr["r_s"][:], func=AF.Exp,
                     scale=-0.5)
    ra = P.add("scalar", 2, rstd)

    P.on("vector", lambda v, n=ra: v.wait_ge(P.chs["scalar"], n))
    vb = P.add("vector", 1, lambda v: v.scalar_tensor_tensor(
        out=scr["b_s"][:], in0=scr["m_s"][:], scalar=C["neg1"][:],
        in1=scr["r_s"][:], op0=OP.mult, op1=OP.mult))

    def bcast(t):
        t.wait_ge(P.chs["vector"], vb)
        for qb in range(nqb):
            t.matmul(ps["A"][:, qb * 512:(qb + 1) * 512], C["ones_row_f"][:],
                     scr["r_s"][0:1, qb * 512:(qb + 1) * 512],
                     start=True, stop=True).then_inc(s_m.h, 1)
            t.matmul(ps["B"][:, qb * 512:(qb + 1) * 512], C["ones_row_f"][:],
                     scr["b_s"][0:1, qb * 512:(qb + 1) * 512],
                     start=True, stop=True).then_inc(s_m.h, 1)
    P.on("tensor", bcast)
    s_m.inc(2 * nqb)
    mm_bc = s_m.n

    P.on("vector", lambda v, n=mm_bc: v.wait_ge(s_m.h, n))

    def getbc(v):
        v.tensor_copy(out=scr["ab_b"][:], in_=ps["A"][:, 0:q])
        v.tensor_copy(out=scr["bb_b"][:], in_=ps["B"][:, 0:q])
    P.add("vector", 2, getbc)
    assert P.cnt["vector"] == v0 + KT + 5 + 1 + 2

    norm_cnt = []
    norm_dst = []
    for k in range(KT):
        if streaming:
            src2 = slot(KT + k)
            wh, wv = slots.wait_val(KT + k)
            P.on("vector", lambda v, wh=wh, wv=wv: v.wait_ge(wh, wv))
        else:
            src2 = xsrc[:, k * q:(k + 1) * q]
        if out_stream is not None:
            odram, oslots = out_stream
            dst = xn[:, (k % 2) * q:(k % 2) * q + q]
            if k >= 2:
                owh, owv = oslots.wait_val(k - 2)
                P.on("vector", lambda v, owh=owh, owv=owv: v.wait_ge(owh, owv))
        else:
            dst = xn[:, k * q:(k + 1) * q]

        def norm(v, src2=src2, dst=dst):
            v.tensor_tensor(out=scr["lnt"][:, 0:q], in0=src2,
                            in1=scr["ab_b"][:, 0:q], op=OP.mult)
            v.tensor_tensor(out=dst, in0=scr["lnt"][:, 0:q],
                            in1=scr["bb_b"][:, 0:q], op=OP.add)
        norm_cnt.append(P.add("vector", 2, norm))
        norm_dst.append(dst)

    if streaming:
        def sync_stream(sync):
            for i in range(2 * KT):
                j = i - 3
                if j >= 0:
                    if j < KT:
                        sync.wait_ge(P.chs["vector"], vcast[j])
                        sync.wait_ge(P.chs["scalar"], asq[j])
                    else:
                        sync.wait_ge(P.chs["vector"], norm_cnt[j - KT])
                k = i if i < KT else i - KT
                sync.dma_start(out=slot(i), in_=xsrc[k * 128:(k + 1) * 128, :]
                               ).then_inc(slots.issue(i), 16)
                if out_stream is not None and i >= KT:
                    ko = i - KT
                    sync.wait_ge(P.chs["vector"], norm_cnt[ko])
                    sync.dma_start(out=odram[ko * 128:(ko + 1) * 128, :],
                                   in_=norm_dst[ko]).then_inc(oslots.issue(ko), 16)
            if out_stream is not None:
                for i in (KT - 2, KT - 1):
                    wh, wv = oslots.wait_val(i)
                    sync.wait_ge(wh, wv)
        P.on("sync", sync_stream)
    else:
        assert out_stream is None, "out_stream requires streaming input here"
    return P.cnt["vector"]


def alloc_ln_scratch(st, nc, q, tag=""):
    return {
        "lnb": sbuf(st, nc, "lnb" + tag, [128, 2 * q], BF16),
        "lnq": sbuf(st, nc, "lnq" + tag, [128, 2 * q], BF16),
        "lnt": sbuf(st, nc, "lnt" + tag, [128, q], F32),
        "st_s": sbuf(st, nc, "st_s" + tag, [1, 2 * q], F32),
        "m_s": sbuf(st, nc, "m_s" + tag, [1, q], F32),
        "v_s": sbuf(st, nc, "v_s" + tag, [1, q], F32),
        "r_s": sbuf(st, nc, "r_s" + tag, [1, q], F32),
        "b_s": sbuf(st, nc, "b_s" + tag, [1, q], F32),
        "ab_b": sbuf(st, nc, "ab_b" + tag, [128, q], F32),
        "bb_b": sbuf(st, nc, "bb_b" + tag, [128, q], F32),
    }


def std_consts(CC):
    return {
        "ones_col_bf": CC.add("ones_col_bf", np.ones((128, 1), BF)),
        "ones_col_f": CC.add("ones_col_f", np.ones((128, 1), np.float32)),
        "ones_row_f": CC.add("ones_row_f", np.ones((1, 128), np.float32)),
        "idb": CC.add("idb", np.eye(128, dtype=BF)),
        "invn": CC.add("invn", np.full((1, 1), 1.0 / D, np.float32)),
        "eps": CC.add("eps", np.full((1, 1), EPS, np.float32)),
        "neg1": CC.add("neg1", np.full((1, 1), -1.0, np.float32)),
    }



def build_L1():
    nc = bass.Bass("TRN2", target_bir_lowering=False, debug=False)
    xT = nc.dram_tensor("xT", [D, TC], F32, kind="ExternalInput").ap()
    qkvw = nc.dram_tensor("qkvw", [96 * 128, KT * 128], F32, kind="ExternalInput").ap()
    qkvbT = nc.dram_tensor("qkvbT", [128, 96], F32, kind="ExternalInput").ap()
    qa = nc.dram_tensor("qa", [D, TC], BF16, kind="ExternalOutput").ap()
    ka = nc.dram_tensor("ka", [D, TC], BF16, kind="ExternalOutput").ap()
    va = nc.dram_tensor("va", [D, TC], BF16, kind="ExternalOutput").ap()

    P = Prog()
    with contextlib.ExitStack() as st:
        s_ld = Sem(st, nc, "ld")
        s_m = Sem(st, nc, "m")
        s_pj = Sem(st, nc, "pj")
        xsl = DmaSlots(st, nc, "xsl", 3)
        wsl = DmaSlots(st, nc, "wsl", 2)
        osl = DmaSlots(st, nc, "osl", 2)

        CC = Consts(nc, st, P, s_ld)
        C = std_consts(CC)

        xstr = sbuf(st, nc, "xstr", [128, 3 * TC], F32)
        xn = sbuf(st, nc, "xn", [128, KT * TC], BF16)
        scr = alloc_ln_scratch(st, nc, TC)
        bias_s = sbuf(st, nc, "bias_s", [128, 96], F32)
        wt = [sbuf(st, nc, f"wt{i}", [128, KT * 128], BF16) for i in range(2)]
        stg = [sbuf(st, nc, f"stg{i}", [128, TC], BF16) for i in range(2)]

        psA = psum(st, nc, "psA", [128, TC])
        psB = psum(st, nc, "psB", [128, TC])

        CC.emit_loads()
        P.on("sync", lambda sync: sync.dma_start(out=bias_s[:], in_=qkvbT
                                                 ).then_inc(s_ld.h, 16))
        n_ld = s_ld.inc(16)
        for e in ("vector", "scalar", "tensor"):
            P.on(e, lambda h, n=n_ld: h.wait_ge(s_ld.h, n))

        ln_done = emit_ln(P, TC, xT, xn, scr, C, {"A": psA, "B": psB}, s_m,
                          stream=(xstr, xsl))

        NF = 96

        def wload(g):
            for f in range(NF):
                if f >= 2:
                    g.wait_ge(s_pj.h, 64 * (f - 1))
                g.dma_start(out=wt[f % 2][:],
                            in_=qkvw[f * 128:(f + 1) * 128, :]
                            ).then_inc(wsl.issue(f), 16)
        P.on("gpsimd", wload)

        v_evac0 = P.cnt["vector"]

        def proj(t):
            t.wait_ge(P.chs["vector"], ln_done)
            for f in range(NF):
                ps = psA if f % 2 == 0 else psB
                wh, wv = wsl.wait_val(f)
                t.wait_ge(wh, wv)
                if f >= 2:
                    t.wait_ge(P.chs["vector"], v_evac0 + f - 1)
                for qb in range(2):
                    for k in range(KT):
                        t.matmul(ps[:, qb * 512:(qb + 1) * 512],
                                 wt[f % 2][:, k * 128:(k + 1) * 128],
                                 xn[:, k * TC + qb * 512: k * TC + qb * 512 + 512],
                                 start=(k == 0), stop=(k == KT - 1)
                                 ).then_inc(s_pj.h, 1)
        P.on("tensor", proj)
        s_pj.inc(64 * NF)

        for f in range(NF):
            def pre(v, f=f):
                v.wait_ge(s_pj.h, 64 * (f + 1))
                if f >= 2:
                    wh, wv = osl.wait_val(f - 2)
                    v.wait_ge(wh, wv)
            P.on("vector", pre)
            P.add("vector", 1, lambda v, f=f: v.tensor_scalar(
                out=stg[f % 2][:], in0=(psA if f % 2 == 0 else psB)[:, 0:TC],
                scalar1=bias_s[:, f:f + 1], scalar2=None, op0=OP.add))
        assert P.cnt["vector"] == v_evac0 + NF

        def outs(sync):
            for f in range(NF):
                sync.wait_ge(P.chs["vector"], v_evac0 + f + 1)
                dst = qa if f < 32 else (ka if f < 64 else va)
                r0 = (f % 32) * 128
                sync.dma_start(out=dst[r0:r0 + 128, :],
                               in_=stg[f % 2][:]).then_inc(osl.issue(f), 16)
            for i in (94, 95):
                wh, wv = osl.wait_val(i)
                sync.wait_ge(wh, wv)
        P.on("sync", outs)

        chains = {e: st.enter_context(nc.semaphore("chain_" + e))
                  for e in ("vector", "scalar", "gpsimd")}
        with nc.Block() as block:
            P.emit(block, chains)
    return nc



def build_L2():
    nc = bass.Bass("TRN2", target_bir_lowering=False, debug=False)
    xT = nc.dram_tensor("xT", [D, TC], F32, kind="ExternalInput").ap()
    qa = nc.dram_tensor("qa", [D, TC], BF16, kind="ExternalInput").ap()
    kg = nc.dram_tensor("kg", [D, T], BF16, kind="ExternalInput").ap()
    vgp = nc.dram_tensor("vgp", [D, T], BF16, kind="ExternalInput").ap()
    owp = nc.dram_tensor("owp", [D, D], F32, kind="ExternalInput").ap()
    obT = nc.dram_tensor("obT", [128, 32], F32, kind="ExternalInput").ap()
    wbp = nc.dram_tensor("wbp", [D, D], F32, kind="ExternalInput").ap()
    bbT = nc.dram_tensor("bbT", [128, 32], F32, kind="ExternalInput").ap()
    gthr_p = nc.dram_tensor("gthr_p", [128, KT * 9], F32, kind="ExternalInput").ap()
    bthr_in = nc.dram_tensor("bthr_in", [1, 1], F32, kind="ExternalInput").ap()
    awp = nc.dram_tensor("awp", [128, D], F32, kind="ExternalInput").ap()
    bwp = nc.dram_tensor("bwp", [128, D], F32, kind="ExternalInput").ap()
    x2T = nc.dram_tensor("x2T", [D, TC], F32, kind="ExternalOutput").ap()
    xn3 = nc.dram_tensor("xn3", [D, TC], BF16, kind="ExternalOutput").ap()
    x1T = nc.dram_tensor("x1T", [D, TC], F32, kind="Internal").ap()

    P = Prog()
    NH = HEADS
    with contextlib.ExitStack() as st:
        s_ld = Sem(st, nc, "ld")
        s_sc = Sem(st, nc, "sc")
        s_cx = Sem(st, nc, "cx")
        s_ep = Sem(st, nc, "ep")
        s_pj2 = Sem(st, nc, "pj2")
        s_u = Sem(st, nc, "u")
        s_ff = Sem(st, nc, "ff")
        s_m = Sem(st, nc, "m")
        qsl = DmaSlots(st, nc, "qsl", 2)
        ksl = DmaSlots(st, nc, "ksl", 2)
        vsl = DmaSlots(st, nc, "vsl", 2)
        w2sl = DmaSlots(st, nc, "w2sl", 2)
        xr1sl = DmaSlots(st, nc, "xr1sl", 2)
        x1osl = DmaSlots(st, nc, "x1osl", 2)
        ln2sl = DmaSlots(st, nc, "ln2sl", 3)
        wbsl = DmaSlots(st, nc, "wbsl", 2)
        xr2sl = DmaSlots(st, nc, "xr2sl", 2)
        x2osl = DmaSlots(st, nc, "x2osl", 2)
        ln3sl = DmaSlots(st, nc, "ln3sl", 3)
        xn3sl = DmaSlots(st, nc, "xn3sl", 2)
        ffsl = DmaSlots(st, nc, "ffsl", 4)

        CC = Consts(nc, st, P, s_ld)
        C = std_consts(CC)
        C["idf"] = CC.add("idf", np.eye(128, dtype=np.float32))
        e16 = np.zeros((8, 128), BF)
        for e in range(8):
            e16[e, e * 16:(e + 1) * 16] = BF(2.0)
        C["e16"] = CC.add("e16", e16)
        C["onecol"] = CC.add("onecol", np.ones((128, 1), np.float32))
        C["zerocol"] = CC.add("zerocol", np.zeros((128, 1), np.float32))
        C["c0125"] = CC.add("c0125", np.full((128, 1), 0.125, np.float32))
        C["tinycol"] = CC.add("tinycol", np.full((128, 1), 1e-30, np.float32))

        xstr = sbuf(st, nc, "xstr", [128, 3 * TC], F32)
        obs = sbuf(st, nc, "obs", [128, 32], F32)
        bbs = sbuf(st, nc, "bbs", [128, 32], F32)

        psS = psum(st, nc, "psS", [128, 2 * TC])
        psC = psum(st, nc, "psC", [128, TC])
        psD = psum(st, nc, "psD", [128, TC])
        psA = psS.ap()[:, 0:TC] if False else None
        psA = psS[:, 0:TC]
        psB = psS[:, TC:2 * TC]

        CC.emit_loads()

        def loads0(sync):
            sync.dma_start(out=obs[:], in_=obT).then_inc(s_ld.h, 16)
            sync.dma_start(out=bbs[:], in_=bbT).then_inc(s_ld.h, 16)
        P.on("sync", loads0)
        n_ld = s_ld.inc(32)
        for e in ("vector", "scalar", "tensor", "gpsimd"):
            P.on(e, lambda h, n=n_ld: h.wait_ge(s_ld.h, n))

        with contextlib.ExitStack() as stA:
            ctxT = sbuf(stA, nc, "ctxT", [128, NH * TC], BF16)
            recb = sbuf(stA, nc, "recb", [128, TC], F32)
            den_s = sbuf(stA, nc, "den_s", [1, TC], F32)
            rec_s = den_s
            stA1 = contextlib.ExitStack()
            qs = sbuf(stA1, nc, "qs", [128, 2 * TC], BF16)
            ks = sbuf(stA1, nc, "ks", [128, 2 * T], BF16)
            vs = sbuf(stA1, nc, "vs", [128, 2 * T], BF16)
            pT = sbuf(stA1, nc, "pT", [128, 4 * TC], BF16)
            acc_v = sbuf(stA1, nc, "acc_v", [128, TC], F32)
            acc_g = sbuf(stA1, nc, "acc_g", [128, TC], F32)

            def fl_loads(sync):
                for h in range(NH):
                    if h >= 2:
                        sync.wait_ge(s_cx.h, 64 * (h - 1))
                    sync.dma_start(out=qs[:, (h % 2) * TC:(h % 2) * TC + TC],
                                   in_=qa[h * 128:(h + 1) * 128, :]
                                   ).then_inc(qsl.issue(h), 16)
                    sync.dma_start(out=ks[:, (h % 2) * T:(h % 2) * T + T],
                                   in_=kg[h * 128:(h + 1) * 128, :]
                                   ).then_inc(ksl.issue(h), 16)
                    sync.dma_start(out=vs[:, (h % 2) * T:(h % 2) * T + T],
                                   in_=vgp[h * 128:(h + 1) * 128, :]
                                   ).then_inc(vsl.issue(h), 16)
            P.on("sync", fl_loads)

            a_fl0 = P.cnt["scalar"]
            exp_cnt = {}
            den_cnt = {}
            div_cnt = {}

            def sc_bank(kt):
                return psS[:, (kt % 2) * TC:(kt % 2) * TC + TC]

            for h in range(NH):
                hb = h % 2
                q_h = qs[:, hb * TC:hb * TC + TC]
                k_h = ks[:, hb * T:hb * T + T]
                v_h = vs[:, hb * T:hb * T + T]

                P.add("vector", 1, lambda v: v.memset(acc_v[:], 0.0))
                if h > 0:
                    P.on("gpsimd", lambda g, n=div_cnt[h - 1] - 1:
                         g.wait_ge(P.chs["vector"], n))
                P.add("gpsimd", 1, lambda g: g.memset(acc_g[:], 0.0))

                def t_head_pre(t, h=h):
                    for sl, idx in ((qsl, h), (ksl, h), (vsl, h)):
                        wh, wv = sl.wait_val(idx)
                        t.wait_ge(wh, wv)
                    if h > 0:
                        t.wait_ge(P.chs["vector"], div_cnt[h - 1])
                P.on("tensor", t_head_pre)

                for kt in range(65):
                    g_kt = 64 * h + kt

                    def t_step(t, kt=kt, g_kt=g_kt, h=h, k_h=k_h, q_h=q_h, v_h=v_h):
                        if kt < 64:
                            if g_kt >= 2:
                                t.wait_ge(P.chs["scalar"], exp_cnt[g_kt - 2])
                            for qb in range(2):
                                inst = t.matmul(
                                    sc_bank(kt)[:, qb * 512:(qb + 1) * 512],
                                    k_h[:, kt * 128:(kt + 1) * 128],
                                    q_h[:, qb * 512:(qb + 1) * 512],
                                    start=True, stop=True)
                                if qb == 1:
                                    inst.then_inc(s_sc.h, 1)
                        if kt >= 1:
                            kp = kt - 1
                            t.wait_ge(P.chs["scalar"], exp_cnt[64 * h + kp])
                            for qb in range(2):
                                inst = t.matmul(
                                    psC[:, qb * 512:(qb + 1) * 512],
                                    v_h[:, kp * 128:(kp + 1) * 128],
                                    pT[:, (kp % 4) * TC + qb * 512:
                                       (kp % 4) * TC + qb * 512 + 512],
                                    start=(kp == 0), stop=(kp == 63))
                                if qb == 1:
                                    inst.then_inc(s_cx.h, 1)
                    P.on("tensor", t_step)
                    if kt < 64:
                        s_sc.inc()
                    if kt >= 1:
                        s_cx.inc()

                    if kt >= 64:
                        continue
                    def a_pre(s, n=s_sc.n, g_kt=g_kt):
                        s.wait_ge(s_sc.h, n)
                        if g_kt >= 4:
                            e_, n_ = den_cnt[g_kt - 4]
                            s.wait_ge(P.chs[e_], n_)
                    P.on("scalar", a_pre)
                    exp_cnt[g_kt] = P.add("scalar", 1, lambda s, kt=kt:
                        s.activation(out=pT[:, (kt % 4) * TC:(kt % 4) * TC + TC],
                                     in_=sc_bank(kt)[:, 0:TC],
                                     func=AF.Exp, scale=ISQ))

                    eng = "gpsimd" if kt % 3 == 2 else "vector"
                    acc = acc_g if kt % 3 == 2 else acc_v
                    P.on(eng, lambda x, n=exp_cnt[g_kt]:
                         x.wait_ge(P.chs["scalar"], n))
                    den_cnt[g_kt] = (eng, P.add(eng, 1, lambda x, acc=acc, kt=kt:
                        x.tensor_tensor(out=acc[:], in0=acc[:],
                                        in1=pT[:, (kt % 4) * TC:(kt % 4) * TC + TC],
                                        op=OP.add)))

                P.on("vector", lambda v, n=den_cnt[64 * h + 62][1]:
                     v.wait_ge(P.chs["gpsimd"], n))
                comb = P.add("vector", 1, lambda v: v.tensor_tensor(
                    out=acc_v[:], in0=acc_v[:], in1=acc_g[:], op=OP.add))

                def t_den(t, n=comb):
                    t.wait_ge(P.chs["vector"], n)
                    for qb in range(2):
                        t.matmul(psD[0:1, qb * 512:(qb + 1) * 512],
                                 C["ones_col_f"][:],
                                 acc_v[:, qb * 512:(qb + 1) * 512],
                                 start=True, stop=True).then_inc(s_ep.h, 1)
                P.on("tensor", t_den)
                s_ep.inc(2)

                P.on("vector", lambda v, n=s_ep.n: v.wait_ge(s_ep.h, n))

                def v_rec(v):
                    v.tensor_copy(out=den_s[:], in_=psD[0:1, 0:TC])
                    v.reciprocal(out=rec_s[:], in_=den_s[:])
                vrec = P.add("vector", 2, v_rec)

                def t_bc(t, n=vrec):
                    t.wait_ge(P.chs["vector"], n)
                    for qb in range(2):
                        t.matmul(psD[:, qb * 512:(qb + 1) * 512],
                                 C["ones_row_f"][:],
                                 rec_s[0:1, qb * 512:(qb + 1) * 512],
                                 start=True, stop=True).then_inc(s_ep.h, 1)
                P.on("tensor", t_bc)
                s_ep.inc(2)

                P.on("scalar", lambda s, n=s_ep.n: s.wait_ge(s_ep.h, n))
                arecb = P.add("scalar", 1, lambda s: s.activation(
                    out=recb[:], in_=psD[:, 0:TC], func=AF.Identity))

                def v_div_pre(v, n1=arecb, n2=s_cx.n):
                    v.wait_ge(P.chs["scalar"], n1)
                    v.wait_ge(s_cx.h, n2)
                P.on("vector", v_div_pre)
                div_cnt[h] = P.add("vector", 1, lambda v, h=h: v.tensor_tensor(
                    out=ctxT[:, h * TC:(h + 1) * TC], in0=psC[:, 0:TC],
                    in1=recb[:], op=OP.mult))

            stA1.close()
            wt2 = [sbuf(stA, nc, f"wt2{i}", [128, KT * 128], BF16) for i in range(2)]
            xrow = sbuf(stA, nc, "xrow", [128, 2 * TC], F32)
            x1stg = sbuf(stA, nc, "x1stg", [128, 2 * TC], F32)

            def w2load(g):
                g.wait_ge(P.chs["vector"], div_cnt[NH - 1])
                for o in range(32):
                    if o >= 2:
                        g.wait_ge(s_pj2.h, 64 * (o - 1))
                    g.dma_start(out=wt2[o % 2][:],
                                in_=owp[o * 128:(o + 1) * 128, :]
                                ).then_inc(w2sl.issue(o), 16)
            P.on("gpsimd", w2load)


            def t_oproj(t):
                t.wait_ge(P.chs["vector"], div_cnt[NH - 1])
                for o in range(32):
                    wh, wv = w2sl.wait_val(o)
                    t.wait_ge(wh, wv)
                    if o >= 2:
                        t.wait_ge(P.chs["vector"], v_op0 + o - 1)
                    ps = psA if o % 2 == 0 else psB
                    for qb in range(2):
                        for k in range(KT):
                            t.matmul(ps[:, qb * 512:(qb + 1) * 512],
                                     wt2[o % 2][:, k * 128:(k + 1) * 128],
                                     ctxT[:, k * TC + qb * 512:k * TC + qb * 512 + 512],
                                     start=(k == 0), stop=(k == KT - 1)
                                     ).then_inc(s_pj2.h, 1)
            v_op0 = P.cnt["vector"]
            P.on("tensor", t_oproj)
            s_pj2.inc(64 * 32)

            for o in range(32):
                def pre(v, o=o):
                    v.wait_ge(s_pj2.h, 64 * (o + 1))
                    wh, wv = xr1sl.wait_val(o)
                    v.wait_ge(wh, wv)
                    if o >= 2:
                        wh2, wv2 = x1osl.wait_val(o - 2)
                        v.wait_ge(wh2, wv2)
                P.on("vector", pre)
                P.add("vector", 1, lambda v, o=o: v.scalar_tensor_tensor(
                    out=x1stg[:, (o % 2) * TC:(o % 2) * TC + TC],
                    in0=(psA if o % 2 == 0 else psB)[:, 0:TC],
                    scalar=obs[:, o:o + 1],
                    in1=xrow[:, (o % 2) * TC:(o % 2) * TC + TC],
                    op0=OP.add, op1=OP.add))
            assert P.cnt["vector"] == v_op0 + 32

            def xr_and_out(sync):
                sync.wait_ge(P.chs["vector"], div_cnt[NH - 1])
                for o in range(34):
                    if o >= 2:
                        sync.wait_ge(P.chs["vector"], v_op0 + o - 1)
                        sync.dma_start(
                            out=x1T[(o - 2) * 128:(o - 1) * 128, :],
                            in_=x1stg[:, ((o - 2) % 2) * TC:((o - 2) % 2) * TC + TC]
                            ).then_inc(x1osl.issue(o - 2), 16)
                    if o < 32:
                        sync.dma_start(out=xrow[:, (o % 2) * TC:(o % 2) * TC + TC],
                                       in_=xT[o * 128:(o + 1) * 128, :]
                                       ).then_inc(xr1sl.issue(o), 16)
            P.on("sync", xr_and_out)

        with contextlib.ExitStack() as stB:
            xn2 = sbuf(stB, nc, "xn2", [128, KT * TC], BF16)
            scr = alloc_ln_scratch(stB, nc, TC)
            gthr_t = sbuf(stB, nc, "gthr_t", [128, KT * 9], BF16)
            bthr_b = sbuf(stB, nc, "bthr_b", [128, 1], F32)
            aw_t = sbuf(stB, nc, "aw_t", [128, D], BF16)
            bw_t = sbuf(stB, nc, "bw_t", [128, D], BF16)
            wt3 = [sbuf(stB, nc, f"wt3{i}", [128, KT * 128], BF16) for i in range(2)]
            xrow2 = sbuf(stB, nc, "xrow2", [128, 2 * TC], F32)
            x2stg = sbuf(stB, nc, "x2stg", [128, 2 * TC], F32)
            xn3stg = sbuf(stB, nc, "xn3stg", [128, 2 * TC], BF16)
            g_s = sbuf(stB, nc, "g_s", [9, TC], F32)
            gts = sbuf(stB, nc, "gts", [128, 64], F32)
            tht = sbuf(stB, nc, "tht", [128, 8], F32)
            ge_s = sbuf(stB, nc, "ge_s", [128, 64], F32)
            gs8 = sbuf(stB, nc, "gs8", [128, 8], F32)
            gr_s = sbuf(stB, nc, "gr_s", [128, 8], F32)
            eth = sbuf(stB, nc, "eth", [128, 8], F32)
            th1 = sbuf(stB, nc, "th1", [128, 8], F32)
            thr_s = sbuf(stB, nc, "thr_s", [128, 8], F32)
            t1_s = sbuf(stB, nc, "t1_s", [128, 8], F32)
            wn_s = sbuf(stB, nc, "wn_s", [128, 64], F32)
            ws8 = sbuf(stB, nc, "ws8", [128, 8], F32)
            wr_s = sbuf(stB, nc, "wr_s", [128, 8], F32)
            wTb = sbuf(stB, nc, "wTb", [9, TC], BF16)
            wb_s = sbuf(stB, nc, "wb_s", [128, TC], BF16)
            h_s = sbuf(stB, nc, "h_s", [128, TC], BF16)
            mr_s = sbuf(stB, nc, "mr_s", [128, TC], BF16)

            def ffload(g):
                g.wait_ge(s_cx.h, 64 * NH)
                g.wait_ge(s_pj2.h, 64 * 32)
                for i in (30, 31):
                    wh, wv = x1osl.wait_val(i)
                    g.wait_ge(wh, wv)
                g.dma_start(out=gthr_t[:], in_=gthr_p).then_inc(ffsl.issue(0), 16)
                g.dma_start(out=aw_t[:], in_=awp).then_inc(ffsl.issue(1), 16)
                g.dma_start(out=bw_t[:], in_=bwp).then_inc(ffsl.issue(2), 16)
            P.on("gpsimd", ffload)

            def ln2gate(sync):
                for i in (30, 31):
                    wh, wv = x1osl.wait_val(i)
                    sync.wait_ge(wh, wv)
            P.on("sync", ln2gate)

            ln2_done = emit_ln(P, TC, x1T, xn2, scr, C, {"A": psA, "B": psB},
                               s_m, stream=(xstr, ln2sl))

            bthr_s = sbuf(stB, nc, "bthr_s", [1, 1], F32)
            P.on("sync", lambda sync: sync.dma_start(out=bthr_s[:], in_=bthr_in
                                                     ).then_inc(ffsl.issue(3), 16))

            def t_bthr(t):
                wh, wv = ffsl.wait_val(3)
                t.wait_ge(wh, wv)
                t.matmul(psC[:, 0:1], C["ones_row_f"][:], bthr_s[:],
                         start=True, stop=True).then_inc(s_u.h, 1)
            P.on("tensor", t_bthr)
            s_u.inc()
            P.on("vector", lambda v, n=s_u.n: v.wait_ge(s_u.h, n))
            v_bthr = P.add("vector", 1, lambda v: v.tensor_copy(
                out=bthr_b[:], in_=psC[:, 0:1]))

            def t_gate(t, vn=ln2_done):
                t.wait_ge(P.chs["vector"], vn)
                wh, wv = ffsl.wait_val(0)
                t.wait_ge(wh, wv)
                for qb in range(2):
                    for k in range(KT):
                        t.matmul(psD[0:9, qb * 512:(qb + 1) * 512],
                                 gthr_t[:, k * 9:(k + 1) * 9],
                                 xn2[:, k * TC + qb * 512:k * TC + qb * 512 + 512],
                                 start=(k == 0), stop=(k == KT - 1)
                                 ).then_inc(s_u.h, 1)
            P.on("tensor", t_gate)
            s_u.inc(64)
            P.on("vector", lambda v, n=s_u.n: v.wait_ge(s_u.h, n))
            vg_ev = P.add("vector", 1, lambda v: v.tensor_copy(
                out=g_s[:], in_=psD[0:9, 0:TC]))

            def t_gT(t, n=vg_ev):
                t.wait_ge(P.chs["vector"], n)
                for j in range(8):
                    t.transpose(psD[:, j * 128:j * 128 + 9],
                                g_s[0:9, j * 128:(j + 1) * 128],
                                C["idf"][0:9, 0:9]).then_inc(s_u.h, 1)
            P.on("tensor", t_gT)
            s_u.inc(8)
            P.on("vector", lambda v, n=s_u.n: v.wait_ge(s_u.h, n))

            def v_gt(v):
                v.tensor_copy(out=gts[:],
                              in_=psD[:, 0:1024]
                              .rearrange("p (j c) -> p j c", c=128)[:, :, 0:8])
                v.tensor_copy(out=tht[:],
                              in_=psD[:, 0:1024]
                              .rearrange("p (j c) -> p j c", c=128)[:, :, 8:9])
            v_gtc = P.add("vector", 2, v_gt)
            P.on("scalar", lambda s, n=v_gtc: s.wait_ge(P.chs["vector"], n))

            def a_gate(s):
                s.activation(out=ge_s[:], in_=gts[:], func=AF.Exp)
                s.activation(out=eth[:], in_=tht[:], func=AF.Exp,
                             bias=bthr_b[:])
            a_ge = P.add("scalar", 2, a_gate)

            def v_gating(v):
                v.wait_ge(P.chs["scalar"], a_ge)
                v.reduce_sum(out=gs8[:],
                             in_=ge_s[:].rearrange("p (j c) -> p j c", c=8),
                             axis=mybir.AxisListType.X)
                v.reciprocal(out=gr_s[:], in_=gs8[:])
                v.tensor_scalar(out=th1[:], in0=eth[:], scalar1=C["onecol"][:],
                                scalar2=None, op0=OP.add)
                v.reciprocal(out=th1[:], in_=th1[:])
                v.scalar_tensor_tensor(out=thr_s[:], in0=eth[:],
                                       scalar=C["c0125"][:], in1=th1[:],
                                       op0=OP.mult, op1=OP.mult)
            P.add("vector", 5, v_gating)

            nops = 0
            def v_wblocks(v):
                for b in range(8):
                    v.tensor_scalar(out=t1_s[:], in0=ge_s[:, b * 8:(b + 1) * 8],
                                    scalar1=gr_s[:, b:b + 1], scalar2=None,
                                    op0=OP.mult)
                    v.tensor_scalar(out=wn_s[:, b * 8:(b + 1) * 8], in0=t1_s[:],
                                    scalar1=thr_s[:, b:b + 1],
                                    scalar2=C["zerocol"][:],
                                    op0=OP.subtract, op1=OP.max)
            P.add("vector", 16, v_wblocks)

            def v_wnorm(v):
                v.reduce_sum(out=ws8[:],
                             in_=wn_s[:].rearrange("p (j c) -> p j c", c=8),
                             axis=mybir.AxisListType.X)
                v.tensor_scalar(out=ws8[:], in0=ws8[:], scalar1=C["tinycol"][:],
                                scalar2=None, op0=OP.max)
                v.reciprocal(out=wr_s[:], in_=ws8[:])
                for b in range(8):
                    v.tensor_scalar(out=wn_s[:, b * 8:(b + 1) * 8],
                                    in0=wn_s[:, b * 8:(b + 1) * 8],
                                    scalar1=wr_s[:, b:b + 1], scalar2=None,
                                    op0=OP.mult)
            v_wn = P.add("vector", 11, v_wnorm)

            def t_wT(t, n=v_wn):
                t.wait_ge(P.chs["vector"], n)
                for b in range(8):
                    t.transpose(psD[0:8, b * 128:(b + 1) * 128],
                                wn_s[:, b * 8:(b + 1) * 8],
                                C["idf"][:]).then_inc(s_u.h, 1)
            P.on("tensor", t_wT)
            s_u.inc(8)
            P.on("vector", lambda v, n=s_u.n: v.wait_ge(s_u.h, n))
            v_wT = P.add("vector", 1, lambda v: v.tensor_copy(
                out=wTb[0:8, :], in_=psD[0:8, 0:TC]))

            def t_e16(t, n=v_wT):
                t.wait_ge(P.chs["vector"], n)
                for qb in range(2):
                    t.matmul(psD[:, qb * 512:(qb + 1) * 512], C["e16"][:],
                             wTb[0:8, qb * 512:(qb + 1) * 512],
                             start=True, stop=True).then_inc(s_u.h, 1)
            P.on("tensor", t_e16)
            s_u.inc(2)
            P.on("vector", lambda v, n=s_u.n: v.wait_ge(s_u.h, n))
            v_wb = P.add("vector", 1, lambda v: v.tensor_copy(
                out=wb_s[:], in_=psD[:, 0:TC]))

            def t_aw(t, vn=ln2_done):
                wh, wv = ffsl.wait_val(1)
                t.wait_ge(wh, wv)
                t.wait_ge(P.chs["vector"], v_bthr)
                for qb in range(2):
                    for k in range(KT):
                        t.matmul(psC[:, qb * 512:(qb + 1) * 512],
                                 aw_t[:, k * 128:(k + 1) * 128],
                                 xn2[:, k * TC + qb * 512:k * TC + qb * 512 + 512],
                                 start=(k == 0), stop=(k == KT - 1)
                                 ).then_inc(s_u.h, 1)
            P.on("tensor", t_aw)
            s_u.inc(64)
            P.on("vector", lambda v, n=s_u.n: v.wait_ge(s_u.h, n))
            P.add("vector", 1, lambda v: v.tensor_copy(out=h_s[:],
                                                       in_=psC[:, 0:TC]))
            v_mr = P.add("vector", 1, lambda v: v.tensor_tensor(
                out=mr_s[:], in0=h_s[:], in1=wb_s[:], op=OP.mult))

            def wb3load(g):
                for o in range(32):
                    if o >= 2:
                        g.wait_ge(s_ff.h, 66 * (o - 1))
                    g.dma_start(out=wt3[o % 2][:],
                                in_=wbp[o * 128:(o + 1) * 128, :]
                                ).then_inc(wbsl.issue(o), 16)
            P.on("gpsimd", wb3load)

            v_ff0 = P.cnt["vector"]


            def t_base(t, n=v_mr):
                t.wait_ge(P.chs["vector"], n)
                wh0, wv0 = ffsl.wait_val(2)
                t.wait_ge(wh0, wv0)
                for o in range(32):
                    wh, wv = wbsl.wait_val(o)
                    t.wait_ge(wh, wv)
                    if o >= 2:
                        t.wait_ge(P.chs["vector"], v_ff0 + o - 1)
                    ps = psA if o % 2 == 0 else psB
                    for qb in range(2):
                        for k in range(KT + 1):
                            if k < KT:
                                lhs = wt3[o % 2][:, k * 128:(k + 1) * 128]
                                rhs = xn2[:, k * TC + qb * 512:k * TC + qb * 512 + 512]
                            else:
                                lhs = bw_t[:, o * 128:(o + 1) * 128]
                                rhs = mr_s[:, qb * 512:(qb + 1) * 512]
                            t.matmul(ps[:, qb * 512:(qb + 1) * 512], lhs, rhs,
                                     start=(k == 0), stop=(k == KT)
                                     ).then_inc(s_ff.h, 1)
            P.on("tensor", t_base)
            s_ff.inc(66 * 32)

            for o in range(32):
                def pre(v, o=o):
                    v.wait_ge(s_ff.h, 66 * (o + 1))
                    wh, wv = xr2sl.wait_val(o)
                    v.wait_ge(wh, wv)
                    if o >= 2:
                        wh2, wv2 = x2osl.wait_val(o - 2)
                        v.wait_ge(wh2, wv2)
                P.on("vector", pre)
                P.add("vector", 1, lambda v, o=o: v.scalar_tensor_tensor(
                    out=x2stg[:, (o % 2) * TC:(o % 2) * TC + TC],
                    in0=(psA if o % 2 == 0 else psB)[:, 0:TC],
                    scalar=bbs[:, o:o + 1],
                    in1=xrow2[:, (o % 2) * TC:(o % 2) * TC + TC],
                    op0=OP.add, op1=OP.add))
            assert P.cnt["vector"] == v_ff0 + 32

            def xr2_and_out(sync):
                for o in range(34):
                    if o >= 2:
                        sync.wait_ge(P.chs["vector"], v_ff0 + o - 1)
                        sync.dma_start(
                            out=x2T[(o - 2) * 128:(o - 1) * 128, :],
                            in_=x2stg[:, ((o - 2) % 2) * TC:((o - 2) % 2) * TC + TC]
                            ).then_inc(x2osl.issue(o - 2), 16)
                    if o < 32:
                        sync.dma_start(out=xrow2[:, (o % 2) * TC:(o % 2) * TC + TC],
                                       in_=x1T[o * 128:(o + 1) * 128, :]
                                       ).then_inc(xr2sl.issue(o), 16)
            P.on("sync", xr2_and_out)

            def ln3gate(sync):
                for i in (30, 31):
                    wh, wv = x2osl.wait_val(i)
                    sync.wait_ge(wh, wv)
            P.on("sync", ln3gate)

            emit_ln(P, TC, x2T, xn3stg, scr, C, {"A": psC, "B": psD}, s_m,
                    stream=(xstr, ln3sl), out_stream=(xn3, xn3sl))
        chains = {e: st.enter_context(nc.semaphore("chain_" + e))
                  for e in ("vector", "scalar", "gpsimd")}
        with nc.Block() as block:
            P.emit(block, chains)
    return nc



def build_L3():
    nc = bass.Bass("TRN2", target_bir_lowering=False, debug=False)
    xgp = nc.dram_tensor("xgp", [16 * 128, KT * 512], BF16, kind="ExternalInput").ap()
    wq_p = nc.dram_tensor("wq_p", [512, D], F32, kind="ExternalInput").ap()
    wk_p = nc.dram_tensor("wk_p", [128, D], F32, kind="ExternalInput").ap()
    wv_p = nc.dram_tensor("wv_p", [128, D], F32, kind="ExternalInput").ap()
    cosT = nc.dram_tensor("cosT", [128, T], F32, kind="ExternalInput").ap()
    sinT = nc.dram_tensor("sinT", [128, T], F32, kind="ExternalInput").ap()
    ctxb = nc.dram_tensor("ctxb", [512, T], BF16, kind="ExternalOutput").ap()

    P = Prog()
    with contextlib.ExitStack() as st:
        s_ld = Sem(st, nc, "ld")
        s_qk = Sem(st, nc, "qk")
        s_sc = Sem(st, nc, "sc")
        s_cx = Sem(st, nc, "cx")
        s_ep = Sem(st, nc, "ep")
        s_tp = Sem(st, nc, "tp")
        xgsl = DmaSlots(st, nc, "xgsl", 2)
        wsl = DmaSlots(st, nc, "wsl", 3)
        cssl = DmaSlots(st, nc, "cssl", 4)
        cosl = DmaSlots(st, nc, "cosl", 2)

        CC = Consts(nc, st, P, s_ld)
        C = std_consts(CC)
        masks = []
        for j in range(4):
            m = (np.arange(128)[:, None] <= np.arange(512)[None, :] - 128 * j)
            masks.append(CC.add(f"mask{j}", m.astype(BF)))
        p64 = np.zeros((128, 128), BF)
        for mcol in range(128):
            p64[(mcol + 64) % 128, mcol] = BF(1.0)
        C["p64"] = CC.add("p64", p64)

        qT = sbuf(st, nc, "qT", [128, 4 * T], BF16)
        kT = sbuf(st, nc, "kT", [128, T], BF16)
        vT = sbuf(st, nc, "vT", [128, T], BF16)
        pT = sbuf(st, nc, "pT", [128, 2048], BF16)
        acc_v = sbuf(st, nc, "acc_v", [128, 512], F32)
        acc_g = sbuf(st, nc, "acc_g", [128, 512], F32)
        recb = sbuf(st, nc, "recb", [128, 512], F32)
        den_s = sbuf(st, nc, "den_s", [1, 512], F32)
        rec_s = sbuf(st, nc, "rec_s", [1, 512], F32)
        cstg = sbuf(st, nc, "cstg", [128, 2 * 512], BF16)
        stq = contextlib.ExitStack()
        xg = sbuf(stq, nc, "xg", [128, KT * 512], BF16)
        wqs = sbuf(stq, nc, "wqs", [128, 4 * D], BF16)
        wks = sbuf(stq, nc, "wks", [128, D], BF16)
        wvs = sbuf(stq, nc, "wvs", [128, D], BF16)

        psA = psum(st, nc, "psA", [128, 1024])
        psC = psum(st, nc, "psC", [128, 1024])
        psD = psum(st, nc, "psD", [128, 1024])
        psT = psum(st, nc, "psT", [128, 2048], BF16)

        CC.emit_loads()

        def wloads(g):
            g.dma_start(out=wqs[:],
                        in_=wq_p.rearrange("(f p) c -> p f c", p=128)
                        ).then_inc(wsl.issue(0), 16)
            g.dma_start(out=wks[:], in_=wk_p).then_inc(wsl.issue(1), 16)
            g.dma_start(out=wvs[:], in_=wv_p).then_inc(wsl.issue(2), 16)
        P.on("gpsimd", wloads)
        n_ld = s_ld.n
        for e in ("vector", "scalar", "tensor"):
            P.on(e, lambda h, n=n_ld: h.wait_ge(s_ld.h, n))

        v_ev0 = P.cnt["vector"]

        def xgload(sync):
            for tb in range(16):
                for hf in range(2):
                    if tb > 0 and hf == 0:
                        sync.wait_ge(s_qk.h, 6 * KT * tb)
                    sync.dma_start(
                        out=xg[:, hf * 16 * 512:(hf + 1) * 16 * 512],
                        in_=xgp[tb * 128:(tb + 1) * 128,
                                hf * 16 * 512:(hf + 1) * 16 * 512]
                        ).then_inc(xgsl.issue(2 * tb + hf), 16)
        P.on("sync", xgload)

        def t_qkv(t):
            for i in range(3):
                wh, wv = wsl.wait_val(i)
                t.wait_ge(wh, wv)
            accs = [psA[:, 0:512], psA[:, 512:1024], psC[:, 0:512],
                    psC[:, 512:1024], psD[:, 0:512], psD[:, 512:1024]]
            for tb in range(16):
                if tb >= 1:
                    t.wait_ge(P.chs["vector"], v_ev0 + 6 * tb)
                for hf in range(2):
                    wh, wv = xgsl.wait_val(2 * tb + hf)
                    t.wait_ge(wh, wv)
                    for f in range(6):
                        lhs_s = (wqs[:, f * D:(f + 1) * D] if f < 4
                                 else (wks if f == 4 else wvs)[:, 0:D])
                        for k in range(hf * 16, hf * 16 + 16):
                            t.matmul(accs[f],
                                     lhs_s[:, k * 128:(k + 1) * 128],
                                     xg[:, k * 512:(k + 1) * 512],
                                     start=(k == 0), stop=(k == KT - 1)
                                     ).then_inc(s_qk.h, 1)
        P.on("tensor", t_qkv)
        s_qk.inc(6 * KT * 16)

        for tb in range(16):
            P.on("vector", lambda v, n=6 * KT * (tb + 1): v.wait_ge(s_qk.h, n))

            def evacs(v, tb=tb):
                v.tensor_copy(out=qT[:, 0 * T + tb * 512:0 * T + (tb + 1) * 512],
                              in_=psA[:, 0:512])
                v.tensor_copy(out=qT[:, 1 * T + tb * 512:1 * T + (tb + 1) * 512],
                              in_=psA[:, 512:1024])
                v.tensor_copy(out=qT[:, 2 * T + tb * 512:2 * T + (tb + 1) * 512],
                              in_=psC[:, 0:512])
                v.tensor_copy(out=qT[:, 3 * T + tb * 512:3 * T + (tb + 1) * 512],
                              in_=psC[:, 512:1024])
                v.tensor_copy(out=kT[:, tb * 512:(tb + 1) * 512],
                              in_=psD[:, 0:512])
                v.tensor_copy(out=vT[:, tb * 512:(tb + 1) * 512],
                              in_=psD[:, 512:1024])
            P.add("vector", 6, evacs)
        assert P.cnt["vector"] == v_ev0 + 96
        stq.close()
        csb = sbuf(st, nc, "csb", [128, 2 * 2048], F32)
        rp1 = sbuf(st, nc, "rp1", [128, 1024], F32)
        rp2 = sbuf(st, nc, "rp2", [128, 1024], F32)

        def csload(sync):
            sync.wait_ge(s_qk.h, 6 * KT * 16)
            for ch in range(8):
                if ch >= 2:
                    sync.wait_ge(P.chs["vector"], v_rp0 + 15 * (ch - 1))
                co = (ch % 2) * 2048
                sync.dma_start(out=csb[:, co:co + 1024],
                               in_=cosT[:, ch * 1024:(ch + 1) * 1024]
                               ).then_inc(cssl.issue(2 * ch), 16)
                sync.dma_start(out=csb[:, co + 1024:co + 2048],
                               in_=sinT[:, ch * 1024:(ch + 1) * 1024]
                               ).then_inc(cssl.issue(2 * ch + 1), 16)
        v_rp0 = P.cnt["vector"]
        P.on("sync", csload)

        s_rp = Sem(st, nc, "rp")
        rope_mm = 0
        rope_cnt = []
        for ch in range(8):
            co = (ch % 2) * 2048
            cosch = csb[:, co:co + 1024]
            sinch = csb[:, co + 1024:co + 2048]
            for i in (2 * ch, 2 * ch + 1):
                wh, wvv = cssl.wait_val(i)
                P.on("vector", lambda v, wh=wh, wvv=wvv: v.wait_ge(wh, wvv))
            for ti in range(5):
                it = 5 * ch + ti
                tile = (qT[:, ti * T + ch * 1024:ti * T + (ch + 1) * 1024]
                        if ti < 4 else kT[:, ch * 1024:(ch + 1) * 1024])
                bank = psA if it % 2 == 0 else psC

                def t_rot(t, X=tile, bank=bank, it=it):
                    if it >= 2:
                        t.wait_ge(P.chs["vector"], rope_cnt[it - 2])
                    else:
                        t.wait_ge(P.chs["vector"], v_ev0 + 96)
                    for half in range(2):
                        t.matmul(bank[:, half * 512:(half + 1) * 512],
                                 C["p64"][:], X[:, half * 512:(half + 1) * 512],
                                 start=True, stop=True).then_inc(s_rp.h, 1)
                P.on("tensor", t_rot)
                rope_mm += 2
                s_rp.inc(2)

                P.on("vector", lambda v, n=s_rp.n: v.wait_ge(s_rp.h, n))

                def rope(v, X=tile, bank=bank, cosch=cosch, sinch=sinch):
                    v.tensor_tensor(out=rp1[:], in0=X, in1=cosch, op=OP.mult)
                    v.tensor_tensor(out=rp2[:], in0=bank[:, 0:1024],
                                    in1=sinch, op=OP.mult)
                    v.tensor_tensor(out=X, in0=rp1[:], in1=rp2[:], op=OP.add)
                rope_cnt.append(P.add("vector", 3, rope))
        v_rope_done = P.cnt["vector"]
        assert v_rope_done == v_rp0 + 120

        a_vt0 = P.cnt["scalar"]

        def t_vt(t):
            t.wait_ge(P.chs["vector"], v_ev0 + 96)
            for kt in range(64):
                if kt >= 2:
                    t.wait_ge(P.chs["scalar"], a_vt0 + kt - 1)
                t.transpose(psT[:, (kt % 2) * 1024:(kt % 2) * 1024 + 128],
                            vT[:, kt * 128:(kt + 1) * 128],
                            C["idb"][:]).then_inc(s_tp.h, 1)
        P.on("tensor", t_vt)
        s_tp.inc(64)

        for kt in range(64):
            P.on("scalar", lambda s, n=kt + 1: s.wait_ge(s_tp.h, n))
            P.add("scalar", 1, lambda s, kt=kt: s.activation(
                out=vT[:, kt * 128:(kt + 1) * 128],
                in_=psT[:, (kt % 2) * 1024:(kt % 2) * 1024 + 128],
                func=AF.Identity))
        a_vt_done = P.cnt["scalar"]
        assert a_vt_done == a_vt0 + 64

        exp_cnt = {}
        den_cnt = {}
        mask_cnt = {}
        div_cnt = {}
        prev_scpair = []
        arecb_hist = {}
        pair_hist = []

        P.on("tensor", lambda t: t.wait_ge(P.chs["vector"], v_rope_done))
        P.on("tensor", lambda t: t.wait_ge(P.chs["scalar"], a_vt_done))

        exp_cnt = {}
        den_cnt = {}
        mask_cnt = {}
        div_cnt = {}
        arecb_hist = {}
        gidx = 0
        prev_last = None

        def sc_bank(kt):
            return psA[:, (kt % 2) * 512:(kt % 2) * 512 + 512]

        def pslot(kt):
            return pT[:, (kt % 4) * 512:(kt % 4) * 512 + 512]

        blocks = [(h, g) for h in range(4) for g in range(16)]
        for h, g in blocks:
            nkt = 4 * g + 4
            P.add("vector", 1, lambda v: v.memset(acc_v[:], 0.0))
            if gidx > 0:
                P.on("gpsimd", lambda gg, n=div_cnt[gidx - 1] - 1:
                     gg.wait_ge(P.chs["vector"], n))
            P.add("gpsimd", 1, lambda gg: gg.memset(acc_g[:], 0.0))

            if gidx > 0:
                P.on("tensor", lambda t, n=div_cnt[gidx - 1]:
                     t.wait_ge(P.chs["vector"], n))

            def gate_of(key):
                if key in mask_cnt:
                    return ("vector", mask_cnt[key])
                return ("scalar", exp_cnt[key])

            for kt in range(nkt + 1):
                def t_step(t, kt=kt, g=g, h=h, gi=gidx, nkt=nkt, pl=prev_last):
                    if kt < nkt:
                        pk = None
                        if kt >= 2:
                            pk = (gi, kt - 2)
                        elif pl is not None:
                            pgi, pnkt = pl
                            pk = (pgi, pnkt - 2 + kt)
                        if pk is not None:
                            t.wait_ge(P.chs["scalar"], exp_cnt[pk])
                        t.matmul(sc_bank(kt),
                                 kT[:, kt * 128:(kt + 1) * 128],
                                 qT[:, h * T + g * 512:h * T + (g + 1) * 512],
                                 start=True, stop=True).then_inc(s_sc.h, 1)
                    if kt >= 1:
                        kp = kt - 1
                        ge, gn = gate_of((gi, kp))
                        t.wait_ge(P.chs[ge], gn)
                        t.matmul(psC[:, (g % 2) * 512:(g % 2) * 512 + 512],
                                 vT[:, kp * 128:(kp + 1) * 128], pslot(kp),
                                 start=(kp == 0), stop=(kp == nkt - 1)
                                 ).then_inc(s_cx.h, 1)
                P.on("tensor", t_step)
                if kt < nkt:
                    s_sc.inc()
                if kt >= 1:
                    s_cx.inc()

                if kt >= nkt:
                    continue
                key = (gidx, kt)

                def a_pre(s, n=s_sc.n, kt=kt, gi=gidx, pl=prev_last, nkt=nkt):
                    s.wait_ge(s_sc.h, n)
                    pk = None
                    if kt >= 4:
                        pk = (gi, kt - 4)
                    elif pl is not None:
                        pgi, pnkt = pl
                        pk = (pgi, pnkt - 4 + kt)
                    if pk is not None and pk in den_cnt:
                        e_, n_ = den_cnt[pk]
                        s.wait_ge(P.chs[e_], n_)
                P.on("scalar", a_pre)
                exp_cnt[key] = P.add("scalar", 1, lambda s, kt=kt:
                    s.activation(out=pslot(kt), in_=sc_bank(kt),
                                 func=AF.Exp, scale=ISQ))

                j = kt - 4 * g
                if j >= 0:
                    P.on("vector", lambda v, n=exp_cnt[key]:
                         v.wait_ge(P.chs["scalar"], n))
                    mask_cnt[key] = P.add("vector", 1,
                        lambda v, kt=kt, j=j: v.tensor_tensor(
                            out=pslot(kt), in0=pslot(kt), in1=masks[j][:],
                            op=OP.mult))

                eng = "gpsimd" if kt % 3 == 2 else "vector"
                acc = acc_g if kt % 3 == 2 else acc_v
                ge, gn = gate_of(key)
                if not (ge == eng == "vector"):
                    P.on(eng, lambda x, ge=ge, gn=gn: x.wait_ge(P.chs[ge], gn))
                den_cnt[key] = (eng, P.add(eng, 1, lambda x, acc=acc, kt=kt:
                    x.tensor_tensor(out=acc[:], in0=acc[:], in1=pslot(kt),
                                    op=OP.add)))

            last_gp = max(k for k in range(nkt) if k % 3 == 2)
            P.on("vector", lambda v, n=den_cnt[(gidx, last_gp)][1]:
                 v.wait_ge(P.chs["gpsimd"], n))
            comb = P.add("vector", 1, lambda v: v.tensor_tensor(
                out=acc_v[:], in0=acc_v[:], in1=acc_g[:], op=OP.add))

            def t_den(t, n=comb, na=arecb_hist.get(gidx - 1)):
                t.wait_ge(P.chs["vector"], n)
                if na is not None:
                    t.wait_ge(P.chs["scalar"], na)
                t.matmul(psD[0:1, 0:512], C["ones_col_f"][:], acc_v[:],
                         start=True, stop=True).then_inc(s_ep.h, 1)
            P.on("tensor", t_den)
            s_ep.inc()

            P.on("vector", lambda v, n=s_ep.n: v.wait_ge(s_ep.h, n))

            def v_rec(v):
                v.tensor_copy(out=den_s[:], in_=psD[0:1, 0:512])
                v.reciprocal(out=rec_s[:], in_=den_s[:])
            vrec = P.add("vector", 2, v_rec)

            def t_bc(t, n=vrec):
                t.wait_ge(P.chs["vector"], n)
                t.matmul(psD[:, 0:512], C["ones_row_f"][:], rec_s[:],
                         start=True, stop=True).then_inc(s_ep.h, 1)
            P.on("tensor", t_bc)
            s_ep.inc()

            P.on("scalar", lambda s, n=s_ep.n: s.wait_ge(s_ep.h, n))
            arecb_hist[gidx] = P.add("scalar", 1, lambda s: s.activation(
                out=recb[:], in_=psD[:, 0:512], func=AF.Identity))

            def v_div_pre(v, n1=arecb_hist[gidx], n2=s_cx.n, gi=gidx):
                v.wait_ge(P.chs["scalar"], n1)
                v.wait_ge(s_cx.h, n2)
                if gi >= 2:
                    wh, wv = cosl.wait_val(gi - 2)
                    v.wait_ge(wh, wv)
            P.on("vector", v_div_pre)
            div_cnt[gidx] = P.add("vector", 1, lambda v, g=g, gi=gidx:
                v.tensor_tensor(out=cstg[:, (gi % 2) * 512:(gi % 2) * 512 + 512],
                                in0=psC[:, (g % 2) * 512:(g % 2) * 512 + 512],
                                in1=recb[:], op=OP.mult))

            def outd(sync, h=h, g=g, gi=gidx, n=div_cnt[gidx]):
                sync.wait_ge(P.chs["vector"], n)
                sync.dma_start(out=ctxb[h * 128:(h + 1) * 128,
                                        g * 512:(g + 1) * 512],
                               in_=cstg[:, (gi % 2) * 512:(gi % 2) * 512 + 512]
                               ).then_inc(cosl.issue(gi), 16)
            P.on("sync", outd)
            prev_last = (gidx, nkt)
            gidx += 1

        def waitout(sync):
            for i in (gidx - 2, gidx - 1):
                wh, wv = cosl.wait_val(i)
                sync.wait_ge(wh, wv)
        P.on("sync", waitout)

        chains = {e: st.enter_context(nc.semaphore("chain_" + e))
                  for e in ("vector", "scalar", "gpsimd")}
        with nc.Block() as block:
            P.emit(block, chains)
    return nc



def build_L4():
    nc = bass.Bass("TRN2", target_bir_lowering=False, debug=False)
    x2T = nc.dram_tensor("x2T", [D, TC], F32, kind="ExternalInput").ap()
    cg = nc.dram_tensor("cg", [D, TC], BF16, kind="ExternalInput").ap()
    wo_p = nc.dram_tensor("wo_p", [D, D], F32, kind="ExternalInput").ap()
    wbp = nc.dram_tensor("wbp", [D, D], F32, kind="ExternalInput").ap()
    bbT = nc.dram_tensor("bbT", [128, 32], F32, kind="ExternalInput").ap()
    gthr_p = nc.dram_tensor("gthr_p", [128, KT * 9], F32, kind="ExternalInput").ap()
    bthr_in = nc.dram_tensor("bthr_in", [1, 1], F32, kind="ExternalInput").ap()
    awp = nc.dram_tensor("awp", [128, D], F32, kind="ExternalInput").ap()
    bwp = nc.dram_tensor("bwp", [128, D], F32, kind="ExternalInput").ap()
    x4T = nc.dram_tensor("x4T", [D, TC], F32, kind="ExternalOutput").ap()
    x3T = nc.dram_tensor("x3T", [D, TC], F32, kind="Internal").ap()

    P = Prog()
    with contextlib.ExitStack() as st:
        s_ld = Sem(st, nc, "ld")
        s_pj2 = Sem(st, nc, "pj2")
        s_u = Sem(st, nc, "u")
        s_ff = Sem(st, nc, "ff")
        s_m = Sem(st, nc, "m")
        cgsl = DmaSlots(st, nc, "cgsl", 2)
        w2sl = DmaSlots(st, nc, "w2sl", 2)
        xr1sl = DmaSlots(st, nc, "xr1sl", 2)
        x3osl = DmaSlots(st, nc, "x3osl", 2)
        ln4sl = DmaSlots(st, nc, "ln4sl", 3)
        wbsl = DmaSlots(st, nc, "wbsl", 2)
        xr2sl = DmaSlots(st, nc, "xr2sl", 2)
        x4osl = DmaSlots(st, nc, "x4osl", 2)
        ffsl = DmaSlots(st, nc, "ffsl", 4)

        CC = Consts(nc, st, P, s_ld)
        C = std_consts(CC)
        C["idf"] = CC.add("idf", np.eye(128, dtype=np.float32))
        e16 = np.zeros((8, 128), BF)
        for e in range(8):
            e16[e, e * 16:(e + 1) * 16] = BF(2.0)
        C["e16"] = CC.add("e16", e16)
        C["onecol"] = CC.add("onecol", np.ones((128, 1), np.float32))
        C["zerocol"] = CC.add("zerocol", np.zeros((128, 1), np.float32))
        C["c0125"] = CC.add("c0125", np.full((128, 1), 0.125, np.float32))
        C["tinycol"] = CC.add("tinycol", np.full((128, 1), 1e-30, np.float32))

        xstr = sbuf(st, nc, "xstr", [128, 3 * TC], F32)
        bbs = sbuf(st, nc, "bbs", [128, 32], F32)
        stO = contextlib.ExitStack()
        cgs = sbuf(stO, nc, "cgs", [128, KT * TC], BF16)
        wt2 = [sbuf(stO, nc, f"wt2{i}", [128, KT * 128], BF16) for i in range(2)]
        xrow = sbuf(stO, nc, "xrow", [128, 2 * TC], F32)
        x3stg = sbuf(stO, nc, "x3stg", [128, 2 * TC], F32)

        psA = psum(st, nc, "psA", [128, TC])
        psB = psum(st, nc, "psB", [128, TC])
        psC = psum(st, nc, "psC", [128, TC])
        psD = psum(st, nc, "psD", [128, TC])

        CC.emit_loads()

        def loads0(sync):
            sync.dma_start(out=bbs[:], in_=bbT).then_inc(s_ld.h, 16)
            sync.dma_start(out=cgs[:, 0:16 * TC],
                           in_=cg[0:16 * 128, :].rearrange("(k p) t -> p k t", p=128)
                           ).then_inc(s_ld.h, 16)
            sync.dma_start(out=cgs[:, 16 * TC:],
                           in_=cg[16 * 128:, :].rearrange("(k p) t -> p k t", p=128)
                           ).then_inc(s_ld.h, 16)
        P.on("sync", loads0)
        n_ld = s_ld.inc(48)
        for e in ("vector", "scalar", "tensor", "gpsimd"):
            P.on(e, lambda h, n=n_ld: h.wait_ge(s_ld.h, n))

        def w2load(g):
            for o in range(32):
                if o >= 2:
                    g.wait_ge(s_pj2.h, 64 * (o - 1))
                g.dma_start(out=wt2[o % 2][:],
                            in_=wo_p[o * 128:(o + 1) * 128, :]
                            ).then_inc(w2sl.issue(o), 16)
        P.on("gpsimd", w2load)

        v_op0 = P.cnt["vector"]

        def t_oproj(t):
            for o in range(32):
                wh, wv = w2sl.wait_val(o)
                t.wait_ge(wh, wv)
                if o >= 2:
                    t.wait_ge(P.chs["vector"], v_op0 + o - 1)
                ps = psA if o % 2 == 0 else psB
                for qb in range(2):
                    for k in range(KT):
                        t.matmul(ps[:, qb * 512:(qb + 1) * 512],
                                 wt2[o % 2][:, k * 128:(k + 1) * 128],
                                 cgs[:, k * TC + qb * 512:k * TC + qb * 512 + 512],
                                 start=(k == 0), stop=(k == KT - 1)
                                 ).then_inc(s_pj2.h, 1)
        P.on("tensor", t_oproj)
        s_pj2.inc(64 * 32)

        for o in range(32):
            def pre(v, o=o):
                v.wait_ge(s_pj2.h, 64 * (o + 1))
                wh, wv = xr1sl.wait_val(o)
                v.wait_ge(wh, wv)
                if o >= 2:
                    wh2, wv2 = x3osl.wait_val(o - 2)
                    v.wait_ge(wh2, wv2)
            P.on("vector", pre)
            P.add("vector", 1, lambda v, o=o: v.tensor_tensor(
                out=x3stg[:, (o % 2) * TC:(o % 2) * TC + TC],
                in0=(psA if o % 2 == 0 else psB)[:, 0:TC],
                in1=xrow[:, (o % 2) * TC:(o % 2) * TC + TC], op=OP.add))
        assert P.cnt["vector"] == v_op0 + 32

        def xr_and_out(sync):
            for o in range(34):
                if o >= 2:
                    sync.wait_ge(P.chs["vector"], v_op0 + o - 1)
                    sync.dma_start(
                        out=x3T[(o - 2) * 128:(o - 1) * 128, :],
                        in_=x3stg[:, ((o - 2) % 2) * TC:((o - 2) % 2) * TC + TC]
                        ).then_inc(x3osl.issue(o - 2), 16)
                if o < 32:
                    sync.dma_start(out=xrow[:, (o % 2) * TC:(o % 2) * TC + TC],
                                   in_=x2T[o * 128:(o + 1) * 128, :]
                                   ).then_inc(xr1sl.issue(o), 16)
        P.on("sync", xr_and_out)

        stO.close()

        with contextlib.ExitStack() as stB:
            xn2 = sbuf(stB, nc, "xn2", [128, KT * TC], BF16)
            scr = alloc_ln_scratch(stB, nc, TC)
            gthr_t = sbuf(stB, nc, "gthr_t", [128, KT * 9], BF16)
            bthr_b = sbuf(stB, nc, "bthr_b", [128, 1], F32)
            aw_t = sbuf(stB, nc, "aw_t", [128, D], BF16)
            bw_t = sbuf(stB, nc, "bw_t", [128, D], BF16)
            wt3 = [sbuf(stB, nc, f"wt3{i}", [128, KT * 128], BF16) for i in range(2)]
            xrow2 = sbuf(stB, nc, "xrow2", [128, 2 * TC], F32)
            x2stg = sbuf(stB, nc, "x2stg", [128, 2 * TC], F32)
            g_s = sbuf(stB, nc, "g_s", [9, TC], F32)
            gts = sbuf(stB, nc, "gts", [128, 64], F32)
            tht = sbuf(stB, nc, "tht", [128, 8], F32)
            ge_s = sbuf(stB, nc, "ge_s", [128, 64], F32)
            gs8 = sbuf(stB, nc, "gs8", [128, 8], F32)
            gr_s = sbuf(stB, nc, "gr_s", [128, 8], F32)
            eth = sbuf(stB, nc, "eth", [128, 8], F32)
            th1 = sbuf(stB, nc, "th1", [128, 8], F32)
            thr_s = sbuf(stB, nc, "thr_s", [128, 8], F32)
            t1_s = sbuf(stB, nc, "t1_s", [128, 8], F32)
            wn_s = sbuf(stB, nc, "wn_s", [128, 64], F32)
            ws8 = sbuf(stB, nc, "ws8", [128, 8], F32)
            wr_s = sbuf(stB, nc, "wr_s", [128, 8], F32)
            wTb = sbuf(stB, nc, "wTb", [9, TC], BF16)
            wb_s = sbuf(stB, nc, "wb_s", [128, TC], BF16)
            h_s = sbuf(stB, nc, "h_s", [128, TC], BF16)
            mr_s = sbuf(stB, nc, "mr_s", [128, TC], BF16)

            def ffload(g):
                g.wait_ge(s_pj2.h, 64 * 32)
                g.dma_start(out=gthr_t[:], in_=gthr_p).then_inc(ffsl.issue(0), 16)
                g.dma_start(out=aw_t[:], in_=awp).then_inc(ffsl.issue(1), 16)
                g.dma_start(out=bw_t[:], in_=bwp).then_inc(ffsl.issue(2), 16)
            P.on("gpsimd", ffload)

            def ln4gate(sync):
                for i in (30, 31):
                    wh, wv = x3osl.wait_val(i)
                    sync.wait_ge(wh, wv)
            P.on("sync", ln4gate)

            ln4_done = emit_ln(P, TC, x3T, xn2, scr, C, {"A": psA, "B": psB},
                               s_m, stream=(xstr, ln4sl))

            bthr_s = sbuf(stB, nc, "bthr_s", [1, 1], F32)
            P.on("sync", lambda sync: sync.dma_start(out=bthr_s[:], in_=bthr_in
                                                     ).then_inc(ffsl.issue(3), 16))

            def t_bthr(t):
                wh, wv = ffsl.wait_val(3)
                t.wait_ge(wh, wv)
                t.matmul(psC[:, 0:1], C["ones_row_f"][:], bthr_s[:],
                         start=True, stop=True).then_inc(s_u.h, 1)
            P.on("tensor", t_bthr)
            s_u.inc()
            P.on("vector", lambda v, n=s_u.n: v.wait_ge(s_u.h, n))
            v_bthr = P.add("vector", 1, lambda v: v.tensor_copy(
                out=bthr_b[:], in_=psC[:, 0:1]))

            def t_gate(t, vn=ln4_done):
                t.wait_ge(P.chs["vector"], vn)
                wh, wv = ffsl.wait_val(0)
                t.wait_ge(wh, wv)
                for qb in range(2):
                    for k in range(KT):
                        t.matmul(psD[0:9, qb * 512:(qb + 1) * 512],
                                 gthr_t[:, k * 9:(k + 1) * 9],
                                 xn2[:, k * TC + qb * 512:k * TC + qb * 512 + 512],
                                 start=(k == 0), stop=(k == KT - 1)
                                 ).then_inc(s_u.h, 1)
            P.on("tensor", t_gate)
            s_u.inc(64)
            P.on("vector", lambda v, n=s_u.n: v.wait_ge(s_u.h, n))
            vg_ev = P.add("vector", 1, lambda v: v.tensor_copy(
                out=g_s[:], in_=psD[0:9, 0:TC]))

            def t_gT(t, n=vg_ev):
                t.wait_ge(P.chs["vector"], n)
                for j in range(8):
                    t.transpose(psD[:, j * 128:j * 128 + 9],
                                g_s[0:9, j * 128:(j + 1) * 128],
                                C["idf"][0:9, 0:9]).then_inc(s_u.h, 1)
            P.on("tensor", t_gT)
            s_u.inc(8)
            P.on("vector", lambda v, n=s_u.n: v.wait_ge(s_u.h, n))

            def v_gt(v):
                v.tensor_copy(out=gts[:],
                              in_=psD[:, 0:1024]
                              .rearrange("p (j c) -> p j c", c=128)[:, :, 0:8])
                v.tensor_copy(out=tht[:],
                              in_=psD[:, 0:1024]
                              .rearrange("p (j c) -> p j c", c=128)[:, :, 8:9])
            v_gtc = P.add("vector", 2, v_gt)
            P.on("scalar", lambda s, n=v_gtc: s.wait_ge(P.chs["vector"], n))

            def a_gate(s):
                s.activation(out=ge_s[:], in_=gts[:], func=AF.Exp)
                s.activation(out=eth[:], in_=tht[:], func=AF.Exp,
                             bias=bthr_b[:])
            a_ge = P.add("scalar", 2, a_gate)

            def v_gating(v):
                v.wait_ge(P.chs["scalar"], a_ge)
                v.reduce_sum(out=gs8[:],
                             in_=ge_s[:].rearrange("p (j c) -> p j c", c=8),
                             axis=mybir.AxisListType.X)
                v.reciprocal(out=gr_s[:], in_=gs8[:])
                v.tensor_scalar(out=th1[:], in0=eth[:], scalar1=C["onecol"][:],
                                scalar2=None, op0=OP.add)
                v.reciprocal(out=th1[:], in_=th1[:])
                v.scalar_tensor_tensor(out=thr_s[:], in0=eth[:],
                                       scalar=C["c0125"][:], in1=th1[:],
                                       op0=OP.mult, op1=OP.mult)
            P.add("vector", 5, v_gating)

            def v_wblocks(v):
                for b in range(8):
                    v.tensor_scalar(out=t1_s[:], in0=ge_s[:, b * 8:(b + 1) * 8],
                                    scalar1=gr_s[:, b:b + 1], scalar2=None,
                                    op0=OP.mult)
                    v.tensor_scalar(out=wn_s[:, b * 8:(b + 1) * 8], in0=t1_s[:],
                                    scalar1=thr_s[:, b:b + 1],
                                    scalar2=C["zerocol"][:],
                                    op0=OP.subtract, op1=OP.max)
            P.add("vector", 16, v_wblocks)

            def v_wnorm(v):
                v.reduce_sum(out=ws8[:],
                             in_=wn_s[:].rearrange("p (j c) -> p j c", c=8),
                             axis=mybir.AxisListType.X)
                v.tensor_scalar(out=ws8[:], in0=ws8[:], scalar1=C["tinycol"][:],
                                scalar2=None, op0=OP.max)
                v.reciprocal(out=wr_s[:], in_=ws8[:])
                for b in range(8):
                    v.tensor_scalar(out=wn_s[:, b * 8:(b + 1) * 8],
                                    in0=wn_s[:, b * 8:(b + 1) * 8],
                                    scalar1=wr_s[:, b:b + 1], scalar2=None,
                                    op0=OP.mult)
            v_wn = P.add("vector", 11, v_wnorm)

            def t_wT(t, n=v_wn):
                t.wait_ge(P.chs["vector"], n)
                for b in range(8):
                    t.transpose(psD[0:8, b * 128:(b + 1) * 128],
                                wn_s[:, b * 8:(b + 1) * 8],
                                C["idf"][:]).then_inc(s_u.h, 1)
            P.on("tensor", t_wT)
            s_u.inc(8)
            P.on("vector", lambda v, n=s_u.n: v.wait_ge(s_u.h, n))
            v_wT = P.add("vector", 1, lambda v: v.tensor_copy(
                out=wTb[0:8, :], in_=psD[0:8, 0:TC]))

            def t_e16(t, n=v_wT):
                t.wait_ge(P.chs["vector"], n)
                for qb in range(2):
                    t.matmul(psD[:, qb * 512:(qb + 1) * 512], C["e16"][:],
                             wTb[0:8, qb * 512:(qb + 1) * 512],
                             start=True, stop=True).then_inc(s_u.h, 1)
            P.on("tensor", t_e16)
            s_u.inc(2)
            P.on("vector", lambda v, n=s_u.n: v.wait_ge(s_u.h, n))
            v_wb = P.add("vector", 1, lambda v: v.tensor_copy(
                out=wb_s[:], in_=psD[:, 0:TC]))

            def t_aw(t, vn=ln4_done):
                wh, wv = ffsl.wait_val(1)
                t.wait_ge(wh, wv)
                t.wait_ge(P.chs["vector"], v_bthr)
                for qb in range(2):
                    for k in range(KT):
                        t.matmul(psC[:, qb * 512:(qb + 1) * 512],
                                 aw_t[:, k * 128:(k + 1) * 128],
                                 xn2[:, k * TC + qb * 512:k * TC + qb * 512 + 512],
                                 start=(k == 0), stop=(k == KT - 1)
                                 ).then_inc(s_u.h, 1)
            P.on("tensor", t_aw)
            s_u.inc(64)
            P.on("vector", lambda v, n=s_u.n: v.wait_ge(s_u.h, n))
            P.add("vector", 1, lambda v: v.tensor_copy(out=h_s[:],
                                                       in_=psC[:, 0:TC]))
            v_mr = P.add("vector", 1, lambda v: v.tensor_tensor(
                out=mr_s[:], in0=h_s[:], in1=wb_s[:], op=OP.mult))

            def wb3load(g):
                g.wait_ge(s_pj2.h, 64 * 32)
                for o in range(32):
                    if o >= 2:
                        g.wait_ge(s_ff.h, 66 * (o - 1))
                    g.dma_start(out=wt3[o % 2][:],
                                in_=wbp[o * 128:(o + 1) * 128, :]
                                ).then_inc(wbsl.issue(o), 16)
            P.on("gpsimd", wb3load)

            v_ff0 = P.cnt["vector"]

            def t_base(t, n=v_mr):
                t.wait_ge(P.chs["vector"], n)
                wh0, wv0 = ffsl.wait_val(2)
                t.wait_ge(wh0, wv0)
                for o in range(32):
                    wh, wv = wbsl.wait_val(o)
                    t.wait_ge(wh, wv)
                    if o >= 2:
                        t.wait_ge(P.chs["vector"], v_ff0 + o - 1)
                    ps = psA if o % 2 == 0 else psB
                    for qb in range(2):
                        for k in range(KT + 1):
                            if k < KT:
                                lhs = wt3[o % 2][:, k * 128:(k + 1) * 128]
                                rhs = xn2[:, k * TC + qb * 512:k * TC + qb * 512 + 512]
                            else:
                                lhs = bw_t[:, o * 128:(o + 1) * 128]
                                rhs = mr_s[:, qb * 512:(qb + 1) * 512]
                            t.matmul(ps[:, qb * 512:(qb + 1) * 512], lhs, rhs,
                                     start=(k == 0), stop=(k == KT)
                                     ).then_inc(s_ff.h, 1)
            P.on("tensor", t_base)
            s_ff.inc(66 * 32)

            for o in range(32):
                def pre(v, o=o):
                    v.wait_ge(s_ff.h, 66 * (o + 1))
                    wh, wv = xr2sl.wait_val(o)
                    v.wait_ge(wh, wv)
                    if o >= 2:
                        wh2, wv2 = x4osl.wait_val(o - 2)
                        v.wait_ge(wh2, wv2)
                P.on("vector", pre)
                P.add("vector", 1, lambda v, o=o: v.scalar_tensor_tensor(
                    out=x2stg[:, (o % 2) * TC:(o % 2) * TC + TC],
                    in0=(psA if o % 2 == 0 else psB)[:, 0:TC],
                    scalar=bbs[:, o:o + 1],
                    in1=xrow2[:, (o % 2) * TC:(o % 2) * TC + TC],
                    op0=OP.add, op1=OP.add))
            assert P.cnt["vector"] == v_ff0 + 32

            def xr2_and_out(sync):
                for o in range(34):
                    if o >= 2:
                        sync.wait_ge(P.chs["vector"], v_ff0 + o - 1)
                        sync.dma_start(
                            out=x4T[(o - 2) * 128:(o - 1) * 128, :],
                            in_=x2stg[:, ((o - 2) % 2) * TC:((o - 2) % 2) * TC + TC]
                            ).then_inc(x4osl.issue(o - 2), 16)
                    if o < 32:
                        sync.dma_start(out=xrow2[:, (o % 2) * TC:(o % 2) * TC + TC],
                                       in_=x3T[o * 128:(o + 1) * 128, :]
                                       ).then_inc(xr2sl.issue(o), 16)
                for i in (30, 31):
                    wh, wv = x4osl.wait_val(i)
                    sync.wait_ge(wh, wv)
            P.on("sync", xr2_and_out)

        chains = {e: st.enter_context(nc.semaphore("chain_" + e))
                  for e in ("vector", "scalar", "gpsimd")}
        with nc.Block() as block:
            P.emit(block, chains)
    return nc



def _rope_tables():
    hd, theta, factor = HD, 500000.0, 8.0
    high_ff, low_ff, orig = 4.0, 1.0, 8192
    inv = 1.0 / theta ** (np.arange(0, hd, 2, dtype=np.float32) / hd)
    wavelen = 2.0 * np.pi / inv
    inv_scaled = np.where(wavelen > orig / low_ff, inv / factor, inv)
    smooth = (orig / wavelen - low_ff) / (high_ff - low_ff)
    smoothed = (1.0 - smooth) / factor * inv + smooth * inv
    is_medium = (wavelen <= orig / low_ff) & (wavelen >= orig / high_ff)
    invf = np.where(is_medium, smoothed, inv_scaled).astype(np.float32)
    pos = np.arange(T, dtype=np.float32)
    freqs = pos[:, None] * invf[None, :]
    emb = np.concatenate([freqs, freqs], axis=1)
    cosT = np.ascontiguousarray(np.cos(emb).T.astype(np.float32))
    sinT = np.ascontiguousarray(np.sin(emb).T.astype(np.float32))
    sinS = sinT.copy()
    sinS[0:64, :] *= -1.0
    return cosT, np.ascontiguousarray(sinS)


def panels(w):
    Dd, Ff = w.shape
    return np.ascontiguousarray(
        w.reshape(Dd // 128, 128, Ff // 128, 128).transpose(2, 1, 0, 3)
        .reshape(Ff // 128 * 128, Dd))


def kernel(**inputs):
    x = np.asarray(inputs["L0_attn_a_x"], np.float32).reshape(T, D)
    qkvw_p = panels(np.asarray(inputs["L0_attn_a_attn_qkvw"], np.float32))
    qkvb = np.asarray(inputs["L0_attn_a_attn_qkvb"], np.float32)
    qkvbT = np.ascontiguousarray(qkvb.reshape(96, 128).T)
    ow = np.asarray(inputs["L0_attn_a_attn_ow"], np.float32)
    ob = np.asarray(inputs["L0_attn_a_attn_ob"], np.float32)
    Wq = np.asarray(inputs["L0_attn_b_Wq"], np.float32)
    Wk = np.asarray(inputs["L0_attn_b_Wk"], np.float32)
    Wv = np.asarray(inputs["L0_attn_b_Wv"], np.float32)
    Wo = np.asarray(inputs["L0_attn_b_Wo"], np.float32)
    W_base = np.asarray(inputs["L0_ffn_W_base"], np.float32)
    b_base = np.asarray(inputs["L0_ffn_b_base"], np.float32)
    W_gate = np.asarray(inputs["L0_ffn_W_gate"], np.float32)
    W_thr = np.asarray(inputs["L0_ffn_W_thr"], np.float32)
    b_thr = np.asarray(inputs["L0_ffn_b_thr"], np.float32)
    A_w = np.asarray(inputs["L0_ffn_A_w"], np.float32)
    B_w = np.asarray(inputs["L0_ffn_B_w"], np.float32)

    cores = list(range(NC))
    xT_c = [np.ascontiguousarray(x[c * TC:(c + 1) * TC].T) for c in cores]

    nc1 = build_L1()
    res1 = run_bass_kernel_spmd(
        nc1, [{"xT": xT_c[c], "qkvw": qkvw_p, "qkvbT": qkvbT} for c in cores],
        core_ids=cores)

    kg = np.ascontiguousarray(np.concatenate(
        [res1.results[c]["ka"] for c in cores], axis=1))
    vg = np.ascontiguousarray(np.concatenate(
        [res1.results[c]["va"] for c in cores], axis=1).T)
    vgp = panels(vg)

    owp = panels(ow)
    obT = np.ascontiguousarray(ob.reshape(32, 128).T)
    wbp = panels(np.ascontiguousarray(W_base.T))
    bbT = np.ascontiguousarray(b_base.reshape(32, 128).T)
    gthr_p = np.ascontiguousarray(
        np.concatenate([W_gate.T, W_thr[:, None]], axis=1)
        .reshape(32, 128, 9).transpose(1, 0, 2).reshape(128, 288))
    bthr_in = b_thr.reshape(1, 1)
    awp = np.ascontiguousarray(
        A_w.reshape(128, 4096).T.reshape(32, 128, 128)
        .transpose(1, 0, 2).reshape(128, 4096))
    bwp = np.ascontiguousarray(B_w.transpose(0, 2, 1).reshape(128, 4096))

    nc2 = build_L2()
    res2 = run_bass_kernel_spmd(
        nc2, [{"xT": xT_c[c], "qa": np.ascontiguousarray(res1.results[c]["qa"]),
               "kg": kg, "vgp": vgp, "owp": owp, "obT": obT, "wbp": wbp,
               "bbT": bbT, "gthr_p": gthr_p, "bthr_in": bthr_in,
               "awp": awp, "bwp": bwp} for c in cores],
        core_ids=cores)

    xn3_full = np.concatenate([res2.results[c]["xn3"] for c in cores], axis=1)
    xgp = np.ascontiguousarray(
        xn3_full.reshape(32, 128, 16, 512).transpose(2, 1, 0, 3)
        .reshape(16 * 128, 32 * 512))
    WqT = np.ascontiguousarray(Wq.T)
    WkT = np.ascontiguousarray(Wk.T)
    WvT = np.ascontiguousarray(Wv.T)
    cosT, sinT = _rope_tables()

    nc3 = build_L3()
    res3 = run_bass_kernel_spmd(
        nc3, [{"xgp": xgp,
               "wq_p": panels(np.ascontiguousarray(WqT[:, c * 512:(c + 1) * 512])),
               "wk_p": panels(np.ascontiguousarray(WkT[:, c * 128:(c + 1) * 128])),
               "wv_p": panels(np.ascontiguousarray(WvT[:, c * 128:(c + 1) * 128])),
               "cosT": cosT, "sinT": sinT} for c in cores],
        core_ids=cores)

    ctx_full = np.concatenate([res3.results[c]["ctxb"] for c in cores], axis=0)
    wo_p = panels(np.ascontiguousarray(Wo.T))

    nc4 = build_L4()
    res4 = run_bass_kernel_spmd(
        nc4, [{"x2T": np.ascontiguousarray(res2.results[c]["x2T"]),
               "cg": np.ascontiguousarray(ctx_full[:, c * TC:(c + 1) * TC]),
               "wo_p": wo_p, "wbp": wbp, "bbT": bbT, "gthr_p": gthr_p,
               "bthr_in": bthr_in, "awp": awp, "bwp": bwp} for c in cores],
        core_ids=cores)

    out = np.concatenate([res4.results[c]["x4T"].T for c in cores], axis=0)
    return np.ascontiguousarray(out, dtype=np.float32).reshape(1, T, D)
